# version 01
# baseline (speedup 1.0000x reference)
import numpy as np
import ml_dtypes
import concourse.bass as bass
import concourse.mybir as mybir
from concourse.bass_utils import run_bass_kernel_spmd

F32 = mybir.dt.float32
BF16 = mybir.dt.bfloat16
AF = mybir.ActivationFunctionType
ALU = mybir.AluOpType
AX = mybir.AxisListType

T = 4096
D = 1024
NF = 8
NT = 32
TRES = 2048


def fm_w(W):
    W = np.asarray(W)
    O, I = W.shape
    nK = max(1, I // 128)
    return np.ascontiguousarray(W.T.reshape(nK, min(I, 128), O).astype(np.float32))

def col(b):
    return np.ascontiguousarray(b.reshape(-1, 128, 1).astype(np.float32))

def row(b):
    return np.ascontiguousarray(b.reshape(1, -1).astype(np.float32))

def fmx(x):
    return np.ascontiguousarray(x.T.reshape(D // 128, 128, x.shape[0]))



class P:
    def __init__(self, nc):
        self.nc = nc
        self.items = []
        self.sb = {}
        self.ps = []

    def item(self, eng, fn, inc=1):
        self.items.append((eng, fn, inc))

    def dma(self, dst, src, eng="sync"):
        self.item(eng, lambda e, d=dst, s=src: e.dma_start(out=d, in_=s), 16)

    def vop(self, fn):
        self.item("vector", fn)

    def sop(self, fn):
        self.item("scalar", fn)

    def mm(self, fn):
        self.item("tensor", fn)

    def emit(self, block, sem):
        items = self.items
        total = sum(i[2] for i in items)

        def emit_for(name, eng):
            v = 0
            for nm, fn, inc in items:
                if nm == name:
                    eng.wait_ge(sem, v)
                    ins = fn(eng)
                    ins.then_inc(sem, inc)
                v += inc
            eng.wait_ge(sem, total)

        @block.sync
        def _(sync):
            emit_for("sync", self.nc.sync)

        @block.vector
        def _(vector):
            emit_for("vector", self.nc.vector)

        @block.scalar
        def _(scalar):
            emit_for("scalar", self.nc.scalar)

        @block.tensor
        def _(tensor):
            emit_for("tensor", self.nc.tensor)

        @block.gpsimd
        def _(gpsimd):
            gpsimd.wait_ge(sem, total)




def prologue(p):
    nc = p.nc
    sb = p.sb
    p.vop(lambda e: e.memset(sb["onescol_f"][:], 1.0))
    p.vop(lambda e: e.memset(sb["onesrow_f"][:], 1.0))
    p.vop(lambda e: e.memset(sb["onescol_b"][:], 1.0))
    p.vop(lambda e: e.memset(sb["onesrow_b"][:], 1.0))


def mm_fm(p, src, nK, wspecs, tres=TRES, src2=None, nK2=0):
    sb = p.sb
    ar = sb["actres"]
    st = sb["stage"]
    ntb = T // tres
    for tb in range(ntb):
        t0 = tb * tres
        srcs = [(src, nK, 0)]
        if src2 is not None:
            srcs.append((src2, nK2, nK))
        for s_ap, s_nk, koff in srcs:
            for k in range(s_nk):
                kp_ = s_ap[k].shape[0]
                p.dma(st[0:kp_, 0:tres], s_ap[k][:, t0:t0 + tres])
                p.vop(lambda e, k=k, koff=koff, kp_=kp_: e.tensor_copy(
                    ar[0:kp_, (koff + k) * tres:(koff + k + 1) * tres], st[0:kp_, 0:tres]))
        for ws in wspecs:
            _mm_one(p, ws, tb, t0, tres)


def _mm_one(p, ws, tb, t0, tres):
    sb = p.sb
    ar = sb["actres"]
    wsb = sb["wsb"]
    wfs = sb["stage2"]
    osb, osb2, rsb = sb["osb"], sb["osb2"], sb["rsb"]
    small = sb["small"]
    nc_ = tres // 512
    O = ws["O"]
    w = ws["w"]
    nK = ws["nK"]
    koff = ws.get("koff", 0)
    func = ws.get("func", None)
    mode = ws.get("mode", "fm")
    bias = ws.get("bias", None)
    ascale = ws.get("ascale", None)
    resid = ws.get("resid", None)
    wscale = ws.get("wscale", None)
    if wscale is not None and tb == 0:
        for k in range(nK):
            p.dma(small[:, 16 + k:17 + k], wscale[k])
    kp = ws.get("kp", 128)
    coef_e = ws.get("coef_e", None)
    if mode == "fm":
        nO = (O + 127) // 128
        for o in range(nO):
            ow = min(128, O - o * 128)
            for k0 in range(0, nK, 4):
                kn = min(4, nK - k0)
                for k in range(k0, k0 + kn):
                    p.dma(wfs[0:kp, (k - k0) * 128:(k - k0) * 128 + ow],
                          w[k][:, o * 128:o * 128 + ow])
                p.vop(lambda e, k0=k0, kn=kn: e.tensor_copy(
                    wsb[0:kp, k0 * 128:(k0 + kn) * 128], wfs[0:kp, 0:kn * 128]))
            if wscale is not None:
                for k in range(nK):
                    p.vop(lambda e, k=k, ow=ow: e.tensor_scalar(
                        wsb[0:kp, k * 128:k * 128 + ow], wsb[0:kp, k * 128:k * 128 + ow],
                        small[:, 16 + k:17 + k], None, ALU.mult))
            if ascale is not None:
                p.dma(small[0:ow, 8:9], ascale[o])
            if bias is not None:
                p.dma(small[0:ow, 9:10], bias[o])
            for c in range(nc_):
                ps = p.ps[c % 4]
                def mmgrp(e, c=c, ps=ps, nK=nK, ow=ow):
                    ins = None
                    for k in range(nK):
                        ins = e.matmul(ps[0:ow, :], wsb[0:kp, k * 128:k * 128 + ow],
                                       ar[0:kp, (koff + k) * tres + c * 512:
                                          (koff + k) * tres + c * 512 + 512],
                                       start=(k == 0), stop=(k == nK - 1))
                    return ins
                p.mm(mmgrp)
                tsrc = ps
                if ascale is not None:
                    p.vop(lambda e, ps=ps, ow=ow: e.tensor_scalar(
                        osb[0:ow, :], ps[0:ow, :], small[0:ow, 8:9], None, ALU.mult))
                    tsrc = osb
                if bias is not None and func is not None:
                    p.sop(lambda e, t=tsrc, ow=ow: e.activation(
                        osb2[0:ow, :], t[0:ow, :], func, bias=small[0:ow, 9:10]))
                elif bias is not None:
                    p.vop(lambda e, t=tsrc, ow=ow: e.tensor_scalar(
                        osb2[0:ow, :], t[0:ow, :], small[0:ow, 9:10], None, ALU.add))
                else:
                    p.sop(lambda e, t=tsrc, ow=ow: e.activation(
                        osb2[0:ow, :], t[0:ow, :], func or AF.Copy))
                if coef_e is not None:
                    p.mm(lambda e, c=c, coef_e=coef_e: e.matmul(
                        p.ps[4][:], sb["bselb"][32:48, coef_e * 128:(coef_e + 1) * 128],
                        sb["growb"][32:48, t0 + c * 512:t0 + c * 512 + 512],
                        start=True, stop=True))
                    p.vop(lambda e, ow=ow: e.tensor_tensor(
                        osb2[0:ow, :], osb2[0:ow, :], p.ps[4][0:ow, :], ALU.mult))
                if resid is not None:
                    rap, rop = resid
                    p.dma(rsb[0:ow, :], rap[o][:, t0 + c * 512:t0 + c * 512 + 512])
                    p.vop(lambda e, rop=rop, ow=ow: e.tensor_tensor(
                        osb2[0:ow, :], osb2[0:ow, :], rsb[0:ow, :], rop))
                p.dma(ws["out"][o][:, t0 + c * 512:t0 + c * 512 + 512], osb2[0:ow, :])
    else:
        nog = O // 512
        brow = sb["brow"]
        if ws.get("bias_row") is not None and tb == 0:
            p.dma(sb["small2"][0:1, 0:O], ws["bias_row"])
            p.vop(lambda e, O=O: e.tensor_copy(brow[0:1, 0:O], sb["small2"][0:1, 0:O]))
        for og in range(nog):
            for k in range(nK):
                p.dma(wfs[:, 0:512], w[k][:, og * 512:(og + 1) * 512])
                p.vop(lambda e, k=k: e.tensor_copy(
                    wsb[:, k * 512:(k + 1) * 512], wfs[:, 0:512]))
                if wscale is not None:
                    p.vop(lambda e, k=k: e.tensor_scalar(
                        wsb[:, k * 512:(k + 1) * 512], wsb[:, k * 512:(k + 1) * 512],
                        small[:, 16 + k:17 + k], None, ALU.mult))
            for tt in range(tres // 128):
                ps = p.ps[tt % 4]
                ttg = (t0 + tt * 128) // 128
                def mmgrp(e, tt=tt, ps=ps, og=og, nK=nK, hasb=(ws.get("bias_row") is not None)):
                    ins = None
                    for k in range(nK):
                        ins = e.matmul(ps[:], ar[:, (koff + k) * tres + tt * 128:
                                                  (koff + k) * tres + tt * 128 + 128],
                                       wsb[:, k * 512:(k + 1) * 512],
                                       start=(k == 0), stop=(k == nK - 1 and not hasb))
                    if hasb:
                        ins = e.matmul(ps[:], sb["onesrow_b"][:],
                                       brow[0:1, og * 512:(og + 1) * 512],
                                       start=False, stop=True)
                    return ins
                p.mm(mmgrp)
                p.sop(lambda e, ps=ps: e.activation(osb2[:], ps[:], ws.get("func") or AF.Copy))
                if resid is not None:
                    rap, rop = resid
                    p.dma(rsb[:], rap[ttg][:, og * 512:(og + 1) * 512])
                    p.vop(lambda e, rop=rop: e.tensor_tensor(osb2[:], osb2[:], rsb[:], rop))
                p.dma(ws["out"][ttg][:, og * 512:(og + 1) * 512], osb2[:])


def trans_fm_tm(p, src, dst, nF=NF):
    sb = p.sb
    st, o = sb["stage"], sb["osb"]
    for tt in range(NT):
        for ft in range(nF):
            p.dma(st[:, 0:128], src[ft][:, tt * 128:(tt + 1) * 128])
            ps = p.ps[ft % 4]
            p.mm(lambda e, ps=ps: e.transpose(ps[:, 0:128], st[:, 0:128], sb["idsb"][:]))
            p.vop(lambda e, ps=ps: e.tensor_copy(o[:], ps[:]))
            p.dma(dst[tt][:, ft * 128:(ft + 1) * 128], o[:, 0:128])


def ln_fm(p, src, dst, nF=NF, eps=1e-5, rms=False, gamma=None, beta=None, tlen=T):
    sb = p.sb
    st, st2 = sb["stage"], sb["stage2"]
    osb, osb2 = sb["osb"], sb["osb2"]
    small = sb["small"]
    Dn = nF * 128.0
    for c in range(tlen // 512):
        c0 = c * 512
        pa, pb = p.ps[4], p.ps[5]
        for ft in range(nF):
            p.dma(st[:, 0:512], src[ft][:, c0:c0 + 512])
            if not rms:
                p.mm(lambda e, ft=ft: e.matmul(pa[0:1, :], sb["onescol_f"][:],
                                               st[:, 0:512], start=(ft == 0), stop=(ft == nF - 1)))
            p.sop(lambda e: e.activation(st2[:, 0:512], st[:, 0:512], AF.Square))
            p.mm(lambda e, ft=ft: e.matmul(pb[0:1, :], sb["onescol_f"][:],
                                           st2[:, 0:512], start=(ft == 0), stop=(ft == nF - 1)))
        if not rms:
            p.sop(lambda e: e.activation(small[0:1, 0:512], pa[0:1, :], AF.Copy, scale=1.0 / Dn))
            p.sop(lambda e: e.activation(small[0:1, 512:1024], small[0:1, 0:512], AF.Square))
            p.sop(lambda e: e.activation(osb[0:1, 0:512], pb[0:1, :], AF.Copy, scale=1.0 / Dn))
            p.vop(lambda e: e.tensor_tensor(osb[0:1, 0:512], osb[0:1, 0:512],
                                            small[0:1, 512:1024], ALU.subtract))
        else:
            p.sop(lambda e: e.activation(osb[0:1, 0:512], pb[0:1, :], AF.Copy, scale=1.0 / Dn))
        p.vop(lambda e: e.tensor_scalar(osb[0:1, 0:512], osb[0:1, 0:512], float(eps), None, ALU.add))
        p.sop(lambda e: e.activation(osb[0:1, 0:512], osb[0:1, 0:512], AF.Ln))
        p.sop(lambda e: e.activation(osb[0:1, 0:512], osb[0:1, 0:512], AF.Exp, scale=-0.5))
        p.mm(lambda e: e.matmul(p.ps[6][:], sb["onesrow_f"][:], osb[0:1, 0:512],
                                start=True, stop=True))
        p.vop(lambda e: e.tensor_copy(st2[:, 512:1024], p.ps[6][:]))
        if not rms:
            p.mm(lambda e: e.matmul(p.ps[6][:], sb["onesrow_f"][:], small[0:1, 0:512],
                                    start=True, stop=True))
        for ft in range(nF):
            p.dma(st[:, 0:512], src[ft][:, c0:c0 + 512])
            if not rms:
                p.vop(lambda e: e.tensor_tensor(osb2[:], st[:, 0:512], p.ps[6][:], ALU.subtract))
                p.vop(lambda e: e.tensor_tensor(osb2[:], osb2[:], st2[:, 512:1024], ALU.mult))
            else:
                p.vop(lambda e: e.tensor_tensor(osb2[:], st[:, 0:512], st2[:, 512:1024], ALU.mult))
            if gamma is not None:
                p.dma(small[:, 8:9], gamma[ft])
                if beta is not None:
                    p.dma(small[:, 9:10], beta[ft])
                    p.vop(lambda e: e.tensor_scalar(osb2[:], osb2[:], small[:, 8:9],
                                                    small[:, 9:10], ALU.mult, ALU.add))
                else:
                    p.vop(lambda e: e.tensor_scalar(osb2[:], osb2[:], small[:, 8:9],
                                                    None, ALU.mult))
            p.dma(dst[ft][:, c0:c0 + 512], osb2[:])


def ew_fm(p, srcs, dst, fn, nF=NF, ch=2048, tlen=T):
    sb = p.sb
    regions = [sb["stage"], sb["stage2"], sb["small"], sb["small2"]]
    for ft in range(nF):
        for c0 in range(0, tlen, ch):
            bufs = []
            for i, s in enumerate(srcs):
                r = regions[i][:, 0:ch]
                p.dma(r, s[ft][:, c0:c0 + ch])
                bufs.append(r)
            fn(p, bufs, ft)
            p.dma(dst[ft][:, c0:c0 + ch], bufs[0])


def attn_qk(p, qfm, kfm, vtm, outfm, scale, causal, mode="softmax",
            denom_eps=0.0, sgfm=None, bsel=None, hret=None, u2b=None):
    sb = p.sb
    ar = sb["actres"]
    st, st2 = sb["stage"], sb["stage2"]
    osb, osb2 = sb["osb"], sb["osb2"]
    small, small2 = sb["small"], sb["small2"]
    kb = ar[:, 0:4096]
    qb = ar[:, 4096:8192]
    vb = ar[:, 8192:10240]
    qe = ar[:, 10240:10752]
    pb1 = ar[:, 10752:11264]
    pb2 = ar[:, 11264:11776]
    use_shift = (mode == "softmax")
    for h in range(16):
        ft, half = h // 2, (h % 2) * 64
        p.dma(st[0:64, 0:2048], kfm[ft][half:half + 64, 0:2048])
        p.vop(lambda e: e.tensor_copy(kb[0:64, 0:2048], st[0:64, 0:2048]))
        p.dma(st[0:64, 0:2048], kfm[ft][half:half + 64, 2048:4096])
        p.vop(lambda e: e.tensor_copy(kb[0:64, 2048:4096], st[0:64, 0:2048]))
        p.vop(lambda e: e.memset(kb[64:65, 0:4096], 1.0))
        for kt in range(NT):
            p.dma(st[:, 0:64], vtm[kt][:, h * 64:h * 64 + 64])
            p.vop(lambda e, kt=kt: e.tensor_copy(vb[:, kt * 64:(kt + 1) * 64], st[:, 0:64]))
        for cc in range(2):
            p.dma(st[0:64, 0:2048], qfm[ft][half:half + 64, cc * 2048:(cc + 1) * 2048])
            if scale != 1.0:
                p.sop(lambda e: e.activation(st[0:64, 0:2048], st[0:64, 0:2048],
                                             AF.Copy, scale=float(scale)))
            p.vop(lambda e, cc=cc: e.tensor_copy(qb[0:64, cc * 2048:(cc + 1) * 2048],
                                                 st[0:64, 0:2048]))
        for qc in range(8):
            q0 = qc * 512
            kt_end = 4 * (qc + 1) if causal else NT
            p.vop(lambda e, q0=q0: e.tensor_copy(qe[0:64, 0:512], qb[0:64, q0:q0 + 512]))
            if use_shift:
                p.vop(lambda e, q0=q0: e.tensor_copy(st2[0:64, 0:512], qb[0:64, q0:q0 + 512]))
                p.sop(lambda e: e.activation(st2[0:64, 0:512], st2[0:64, 0:512], AF.Square))
                p.mm(lambda e: e.matmul(p.ps[4][0:1, :], sb["onescol_f"][0:64, :],
                                        st2[0:64, 0:512], start=True, stop=True))
                p.sop(lambda e: e.activation(small[0:1, 0:512], p.ps[4][0:1, :], AF.Sqrt))
                if qc == 0:
                    p.vop(lambda e: e.tensor_copy(st2[0:64, 0:2048], kb[0:64, 0:2048]))
                    p.sop(lambda e: e.activation(st2[0:64, 0:2048], st2[0:64, 0:2048], AF.Square))
                    p.mm(lambda e: e.matmul(p.ps[5][0:1, :], sb["onescol_f"][0:64, :],
                                            st2[0:64, 0:512], start=True, stop=False))
                    p.mm(lambda e: e.matmul(p.ps[5][0:1, :], sb["onescol_f"][0:64, :],
                                            st2[0:64, 512:1024], start=False, stop=False))
                    p.mm(lambda e: e.matmul(p.ps[5][0:1, :], sb["onescol_f"][0:64, :],
                                            st2[0:64, 1024:1536], start=False, stop=False))
                    p.mm(lambda e: e.matmul(p.ps[5][0:1, :], sb["onescol_f"][0:64, :],
                                            st2[0:64, 1536:2048], start=False, stop=True))
                    p.vop(lambda e: e.reduce_max(small2[0:1, 4:5], p.ps[5][0:1, :], axis=AX.X))
                    p.vop(lambda e: e.tensor_copy(st2[0:64, 0:2048], kb[0:64, 2048:4096]))
                    p.sop(lambda e: e.activation(st2[0:64, 0:2048], st2[0:64, 0:2048], AF.Square))
                    p.mm(lambda e: e.matmul(p.ps[5][0:1, :], sb["onescol_f"][0:64, :],
                                            st2[0:64, 0:512], start=True, stop=False))
                    p.mm(lambda e: e.matmul(p.ps[5][0:1, :], sb["onescol_f"][0:64, :],
                                            st2[0:64, 512:1024], start=False, stop=False))
                    p.mm(lambda e: e.matmul(p.ps[5][0:1, :], sb["onescol_f"][0:64, :],
                                            st2[0:64, 1024:1536], start=False, stop=False))
                    p.mm(lambda e: e.matmul(p.ps[5][0:1, :], sb["onescol_f"][0:64, :],
                                            st2[0:64, 1536:2048], start=False, stop=True))
                    p.vop(lambda e: e.reduce_max(small2[0:1, 5:6], p.ps[5][0:1, :], axis=AX.X))
                    p.vop(lambda e: e.tensor_tensor(small2[0:1, 4:5], small2[0:1, 4:5],
                                                    small2[0:1, 5:6], ALU.max))
                    p.sop(lambda e: e.activation(small2[0:1, 6:7], small2[0:1, 4:5], AF.Sqrt))
                p.vop(lambda e: e.tensor_scalar(small[0:1, 0:512], small[0:1, 0:512],
                                                small2[0:1, 6:7], -1.0, ALU.mult, ALU.mult))
                p.vop(lambda e: e.tensor_copy(qe[64:65, 0:512], small[0:1, 0:512]))
            else:
                p.vop(lambda e: e.memset(qe[64:65, 0:512], 0.0))
            first = True
            for kt in range(kt_end):
                kk = 65 if use_shift else 64
                p.mm(lambda e, kt=kt, kk=kk: e.matmul(
                    p.ps[0][:], kb[0:kk, kt * 128:(kt + 1) * 128], qe[0:kk, 0:512],
                    start=True, stop=True))
                diag = causal and (kt >= 4 * qc)
                if mode == "softmax":
                    p.sop(lambda e: e.activation(pb1[:, 0:512], p.ps[0][:], AF.Exp))
                else:
                    p.vop(lambda e: e.tensor_copy(pb1[:, 0:512], p.ps[0][:]))
                if diag:
                    m = kt - 4 * qc
                    p.vop(lambda e, m=m: e.tensor_tensor(
                        pb1[:, 0:512], pb1[:, 0:512], sb["msb"][:, m * 512:(m + 1) * 512],
                        ALU.mult))
                last = (kt == kt_end - 1)
                p.mm(lambda e, kt=kt, first=first, last=last: e.matmul(
                    p.ps[1][0:64, :], vb[:, kt * 64:(kt + 1) * 64], pb1[:, 0:512],
                    start=first, stop=last))
                p.mm(lambda e, first=first, last=last: e.matmul(
                    p.ps[3][0:1, :], sb["onescol_b"][:], pb1[:, 0:512],
                    start=first, stop=last))
                if mode == "linear":
                    kc = kt // 2
                    p.mm(lambda e, kc=kc, q0=q0: e.matmul(
                        p.ps[4][:], bsel[32:48, kc * 128:(kc + 1) * 128],
                        sgfm[32:48, q0:q0 + 512], start=True, stop=True))
                    p.vop(lambda e: e.tensor_copy(pb2[:, 0:512], p.ps[0][:]))
                    p.vop(lambda e: e.tensor_tensor(pb2[:, 0:512], pb2[:, 0:512],
                                                    p.ps[4][:], ALU.mult))
                    p.mm(lambda e, kt=kt, first=first, last=last: e.matmul(
                        p.ps[2][0:64, :], vb[:, kt * 64:(kt + 1) * 64], pb2[:, 0:512],
                        start=first, stop=last))
                first = False
            p.vop(lambda e, de=float(denom_eps): e.tensor_scalar(
                small[0:1, 0:512], p.ps[3][0:1, :], de, None, ALU.add))
            p.vop(lambda e: e.reciprocal(small[0:1, 512:1024], small[0:1, 0:512]))
            p.mm(lambda e: e.matmul(p.ps[4][0:64, :], sb["onesrow_f"][0:1, 0:64],
                                    small[0:1, 512:1024], start=True, stop=True))
            p.vop(lambda e: e.tensor_copy(st2[0:64, 0:512], p.ps[4][0:64, :]))
            p.vop(lambda e: e.tensor_tensor(osb[0:64, 0:512], p.ps[1][0:64, :],
                                            st2[0:64, 0:512], ALU.mult))
            p.dma(outfm[ft][half:half + 64, q0:q0 + 512], osb[0:64, 0:512])
            if mode == "linear":
                p.vop(lambda e: e.tensor_copy(osb2[0:64, 0:512], p.ps[2][0:64, :]))
                p.dma(hret[ft][half:half + 64, q0:q0 + 512], osb2[0:64, 0:512])


def headnorm_fm(p, src, dst, eps=1e-6):
    sb = p.sb
    st, st2, osb, small = sb["stage"], sb["stage2"], sb["osb"], sb["small"]
    for ft in range(NF):
        for c in range(8):
            c0 = c * 512
            p.dma(st[:, 0:512], src[ft][:, c0:c0 + 512])
            p.sop(lambda e: e.activation(st2[:, 0:512], st[:, 0:512], AF.Square))
            for half in (0, 64):
                p.mm(lambda e, half=half: e.matmul(p.ps[4][0:1, :], sb["onescol_f"][half:half + 64, :],
                                                   st2[half:half + 64, 0:512], start=True, stop=True))
                p.sop(lambda e: e.activation(small[0:1, 0:512], p.ps[4][0:1, :], AF.Sqrt))
                p.vop(lambda e: e.tensor_scalar(small[0:1, 0:512], small[0:1, 0:512], float(eps), None, ALU.add))
                p.vop(lambda e: e.reciprocal(small[0:1, 512:1024], small[0:1, 0:512]))
                p.mm(lambda e: e.matmul(p.ps[5][0:64, :], sb["onesrow_f"][0:1, 0:64],
                                        small[0:1, 512:1024], start=True, stop=True))
                p.vop(lambda e, half=half: e.tensor_tensor(osb[half:half + 64, 0:512],
                                                           st[half:half + 64, 0:512],
                                                           p.ps[5][0:64, :], ALU.mult))
            p.dma(dst[ft][:, c0:c0 + 512], osb[:, 0:512])


def kst_tm(p, src, dst, eps=1e-6, silu=True):
    sb = p.sb
    st, st2, small = sb["stage"], sb["stage2"], sb["small"]
    for tt in range(NT):
        p.dma(st[:, 0:1024], src[tt])
        if silu:
            p.sop(lambda e: e.activation(st[:, 0:1024], st[:, 0:1024], AF.Silu))
            p.vop(lambda e: e.tensor_scalar(st[:, 0:1024], st[:, 0:1024], 0.5, None, ALU.add))
        p.sop(lambda e: e.activation(st2[:, 0:1024], st[:, 0:1024], AF.Square))
        p.vop(lambda e: e.tensor_reduce(small[:, 0:16],
                                        st2[:, 0:1024].rearrange("p (h d) -> p h d", d=64),
                                        axis=AX.X, op=ALU.add))
        p.sop(lambda e: e.activation(small[:, 0:16], small[:, 0:16], AF.Sqrt))
        p.vop(lambda e: e.tensor_scalar(small[:, 0:16], small[:, 0:16], float(eps), None, ALU.add))
        p.vop(lambda e: e.reciprocal(small[:, 16:32], small[:, 0:16]))
        for h in range(16):
            p.vop(lambda e, h=h: e.tensor_scalar(st[:, h * 64:(h + 1) * 64],
                                                 st[:, h * 64:(h + 1) * 64],
                                                 small[:, 16 + h:17 + h], None, ALU.mult))
        p.dma(dst[tt], st[:, 0:1024])


def states_out(p, k0n, v0, k1n, v1, qh, out):
    sb = p.sb
    ar, st, st2, wsb, osb = sb["actres"], sb["stage"], sb["stage2"], sb["wsb"], sb["osb"]
    for tt in range(NT):
        for i, srcb in enumerate((k0n, v0, k1n, v1)):
            p.dma(st[:, 0:1024], srcb[tt])
            p.vop(lambda e, i=i: e.tensor_copy(ar[:, i * 1024:(i + 1) * 1024], st[:, 0:1024]))
        def grp(e, tt=tt):
            ins = None
            for ki in range(2):
                for h in range(16):
                    b = ki * 2 + h // 8
                    cc = (h % 8) * 64
                    ins = e.matmul(p.ps[b][0:64, cc:cc + 64],
                                   ar[:, ki * 2048 + h * 64:ki * 2048 + h * 64 + 64],
                                   ar[:, ki * 2048 + 1024 + h * 64:ki * 2048 + 1024 + h * 64 + 64],
                                   start=(tt == 0 and h % 8 == 0), stop=(tt == NT - 1 and h % 8 == 7))
            return ins
        p.mm(grp)
    sc = float(2.0 ** -12)
    for b in range(4):
        half = b % 2
        dst = st if b < 2 else st2
        p.sop(lambda e, b=b, dst=dst, half=half: e.activation(
            dst[0:64, half * 512:half * 512 + 512], p.ps[b][0:64, :], AF.Copy, scale=sc))
    p.vop(lambda e: e.tensor_tensor(st[0:64, 0:1024], st[0:64, 0:1024], st2[0:64, 0:1024], ALU.mult))
    p.vop(lambda e: e.tensor_copy(wsb[0:64, 0:1024], st[0:64, 0:1024]))
    for h in range(16):
        ft, half = h // 2, (h % 2) * 64
        for cc in range(2):
            p.dma(st[0:64, 0:2048], qh[ft][half:half + 64, cc * 2048:(cc + 1) * 2048])
            p.vop(lambda e, cc=cc: e.tensor_copy(ar[0:64, cc * 2048:(cc + 1) * 2048], st[0:64, 0:2048]))
        for c in range(8):
            p.mm(lambda e, h=h, c=c: e.matmul(p.ps[4][0:64, :], wsb[0:64, h * 64:(h + 1) * 64],
                                              ar[0:64, c * 512:(c + 1) * 512], start=True, stop=True))
            p.vop(lambda e: e.tensor_copy(osb[0:64, 0:512], p.ps[4][0:64, :]))
            p.dma(out[ft][half:half + 64, c * 512:(c + 1) * 512], osb[0:64, 0:512])


def gate_moe(p, src, wg, E, mode, bias_row=None, renorm_eps=1e-6):
    sb = p.sb
    st, st2, small, small2, wsb = sb["stage"], sb["stage2"], sb["small"], sb["small2"], sb["wsb"]
    gb = sb["growb"]
    p.vop(lambda e: e.memset(gb[32:64, :], 0.0))
    for k in range(NF):
        p.dma(st2[:, 0:E], wg[k])
        p.vop(lambda e, k=k: e.tensor_copy(wsb[:, k * E:(k + 1) * E], st2[:, 0:E]))
    if bias_row is not None:
        p.dma(st2[0:1, 0:E], bias_row)
        p.vop(lambda e: e.tensor_copy(sb["brow"][0:1, 0:E], st2[0:1, 0:E]))
    for tt in range(NT):
        for k in range(NF):
            p.dma(st[:, 0:128], src[k][:, tt * 128:(tt + 1) * 128])
            p.vop(lambda e, k=k: e.tensor_copy(sb["actres"][:, k * 128:(k + 1) * 128], st[:, 0:128]))
        def grp(e, hasb=(bias_row is not None)):
            ins = None
            for k in range(NF):
                ins = e.matmul(p.ps[0][:, 0:E], sb["actres"][:, k * 128:(k + 1) * 128],
                               wsb[:, k * E:(k + 1) * E], start=(k == 0),
                               stop=(k == NF - 1 and not hasb))
            if hasb:
                ins = e.matmul(p.ps[0][:, 0:E], sb["onesrow_b"][:], sb["brow"][0:1, 0:E],
                               start=False, stop=True)
            return ins
        p.mm(grp)
        g = small
        p.vop(lambda e: e.tensor_reduce(small2[:, 0:1], p.ps[0][:, 0:E], axis=AX.X,
                                        op=ALU.max, negate=True))
        p.sop(lambda e: e.activation(g[:, 0:E], p.ps[0][:, 0:E], AF.Exp, bias=small2[:, 0:1]))
        p.vop(lambda e: e.tensor_reduce(small2[:, 1:2], g[:, 0:E], axis=AX.X, op=ALU.add))
        p.vop(lambda e: e.reciprocal(small2[:, 2:3], small2[:, 1:2]))
        p.vop(lambda e: e.tensor_scalar(g[:, 0:E], g[:, 0:E], small2[:, 2:3], None, ALU.mult))
        if mode == "top1w":
            p.vop(lambda e: e.tensor_reduce(small2[:, 3:4], g[:, 0:E], axis=AX.X, op=ALU.max))
            p.vop(lambda e: e.tensor_scalar(st2[:, 0:E], g[:, 0:E], small2[:, 3:4], None, ALU.is_ge))
            p.vop(lambda e: e.tensor_scalar(g[:, 0:E], st2[:, 0:E], small2[:, 3:4], None, ALU.mult))
        elif mode == "top1":
            p.vop(lambda e: e.tensor_reduce(small2[:, 3:4], g[:, 0:E], axis=AX.X, op=ALU.max))
            p.vop(lambda e: e.tensor_scalar(g[:, 0:E], g[:, 0:E], small2[:, 3:4], None, ALU.is_ge))
        elif mode == "top2renorm":
            p.vop(lambda e: e.tensor_reduce(small2[:, 3:4], g[:, 0:E], axis=AX.X, op=ALU.max))
            p.vop(lambda e: e.tensor_scalar(st2[:, 0:E], g[:, 0:E], small2[:, 3:4], None, ALU.is_ge))
            p.vop(lambda e: e.scalar_tensor_tensor(st2[:, 0:E], st2[:, 0:E], -1e30, g[:, 0:E],
                                                   ALU.mult, ALU.add))
            p.vop(lambda e: e.tensor_reduce(small2[:, 4:5], st2[:, 0:E], axis=AX.X, op=ALU.max))
            p.vop(lambda e: e.tensor_scalar(st2[:, 0:E], g[:, 0:E], small2[:, 4:5], None, ALU.is_ge))
            p.vop(lambda e: e.tensor_tensor(g[:, 0:E], g[:, 0:E], st2[:, 0:E], ALU.mult))
            p.vop(lambda e: e.tensor_reduce(small2[:, 5:6], g[:, 0:E], axis=AX.X, op=ALU.add))
            p.vop(lambda e: e.tensor_scalar(small2[:, 5:6], small2[:, 5:6], float(renorm_eps), None, ALU.add))
            p.vop(lambda e: e.reciprocal(small2[:, 6:7], small2[:, 5:6]))
            p.vop(lambda e: e.tensor_scalar(g[:, 0:E], g[:, 0:E], small2[:, 6:7], None, ALU.mult))
        p.mm(lambda e: e.transpose(p.ps[2][0:E, 0:128], g[:, 0:E], sb["idsb"][:]))
        p.vop(lambda e, tt=tt: e.tensor_copy(gb[32:32 + E, tt * 128:(tt + 1) * 128],
                                             p.ps[2][0:E, 0:128]))


def l2n_fm(p, src, dst, eps=1e-12, tlen=T):
    sb = p.sb
    st, st2, osb, osb2, small = sb["stage"], sb["stage2"], sb["osb"], sb["osb2"], sb["small"]
    for c in range(tlen // 512):
        c0 = c * 512
        pb = p.ps[5]
        for ft in range(NF):
            p.dma(st[:, 0:512], src[ft][:, c0:c0 + 512])
            p.sop(lambda e: e.activation(st2[:, 0:512], st[:, 0:512], AF.Square))
            p.mm(lambda e, ft=ft: e.matmul(pb[0:1, :], sb["onescol_f"][:],
                                           st2[:, 0:512], start=(ft == 0), stop=(ft == NF - 1)))
        p.sop(lambda e: e.activation(small[0:1, 0:512], pb[0:1, :], AF.Sqrt))
        p.vop(lambda e: e.tensor_scalar(small[0:1, 0:512], small[0:1, 0:512], float(eps), None, ALU.max))
        p.vop(lambda e: e.reciprocal(small[0:1, 512:1024], small[0:1, 0:512]))
        p.mm(lambda e: e.matmul(p.ps[6][:], sb["onesrow_f"][:], small[0:1, 512:1024],
                                start=True, stop=True))
        p.vop(lambda e: e.tensor_copy(st2[:, 0:512], p.ps[6][:]))
        for ft in range(NF):
            p.dma(st[:, 0:512], src[ft][:, c0:c0 + 512])
            p.vop(lambda e: e.tensor_tensor(osb2[:], st[:, 0:512], st2[:, 0:512], ALU.mult))
            p.dma(dst[ft][:, c0:c0 + 512], osb2[:])


def rope_fm(p, src, cs):
    sb = p.sb
    st, st2, small, osb = sb["stage"], sb["stage2"], sb["small"], sb["osb"]
    for h in range(16):
        ft, half = h // 2, (h % 2) * 64
        for c in range(8):
            c0 = c * 512
            p.dma(st[0:64, 0:512], src[ft][half:half + 64, c0:c0 + 512])
            p.dma(st2[0:32, 0:512], cs[0][:, c0:c0 + 512])
            p.dma(st2[32:64, 0:512], cs[1][:, c0:c0 + 512])
            p.dma(st2[0:32, 512:1024], cs[1][:, c0:c0 + 512])
            p.dma(st2[32:64, 512:1024], cs[0][:, c0:c0 + 512])
            p.vop(lambda e: e.tensor_tensor(small[0:32, 0:512], st[0:32, 0:512],
                                            st2[0:32, 0:512], ALU.mult))
            p.vop(lambda e: e.tensor_tensor(small[0:32, 512:1024], st[32:64, 0:512],
                                            st2[32:64, 0:512], ALU.mult))
            p.vop(lambda e: e.tensor_tensor(small[32:64, 0:512], st[0:32, 0:512],
                                            st2[0:32, 512:1024], ALU.mult))
            p.vop(lambda e: e.tensor_tensor(small[32:64, 512:1024], st[32:64, 0:512],
                                            st2[32:64, 512:1024], ALU.mult))
            p.vop(lambda e: e.tensor_tensor(osb[0:32, 0:512], small[0:32, 0:512],
                                            small[0:32, 512:1024], ALU.subtract))
            p.vop(lambda e: e.tensor_tensor(osb[32:64, 0:512], small[32:64, 0:512],
                                            small[32:64, 512:1024], ALU.add))
            p.dma(src[ft][half:half + 64, c0:c0 + 512], osb[0:64, 0:512])



def sb_alloc(nc, stack):
    ent = lambda *a: stack.enter_context(nc.sbuf_tensor(*a))
    sb = {}
    sb["actres"] = ent("actres", [128, 65536], BF16)
    sb["stage"] = ent("stage", [128, 2048], F32)
    sb["stage2"] = ent("stage2", [128, 2048], F32)
    sb["wsb"] = ent("wsb", [128, 4096], BF16)
    sb["osb"] = ent("osb", [128, 512], F32)
    sb["osb2"] = ent("osb2", [128, 512], F32)
    sb["rsb"] = ent("rsb", [128, 512], F32)
    sb["small"] = ent("small", [128, 1024], F32)
    sb["small2"] = ent("small2", [128, 1024], F32)
    sb["growb"] = ent("growb", [64, 4096], BF16)
    sb["brow"] = ent("brow", [1, 1024], BF16)
    sb["onescol_f"] = ent("onescol_f", [128, 1], F32)
    sb["onesrow_f"] = ent("onesrow_f", [1, 128], F32)
    sb["onescol_b"] = ent("onescol_b", [128, 1], BF16)
    sb["onesrow_b"] = ent("onesrow_b", [1, 128], BF16)
    sb["idsb"] = ent("idsb", [128, 128], F32)
    sb["msb"] = ent("msb", [128, 2048], BF16)
    sb["u2b"] = ent("u2b", [17, 16], BF16)
    sb["bselb"] = ent("bselb", [64, 2048], BF16)
    sb["cst"] = ent("cst", [128, 2], F32)
    return sb


def make_prog(specs, emit_fn):
    import contextlib
    nc = bass.Bass("TRN2", target_bir_lowering=False)
    t = {}
    for name, shape, dtype, kind in specs:
        t[name] = nc.dram_tensor(name, list(shape), dtype, kind=kind).ap()
    with contextlib.ExitStack() as stack:
        sb = sb_alloc(nc, stack)
        ps = [stack.enter_context(nc.psum_tensor("ps%d" % i, [128, 512], F32))
              for i in range(8)]
        sem = stack.enter_context(nc.semaphore("chain"))
        block = stack.enter_context(nc.Block())
        p = P(nc)
        p.sb = sb
        p.ps = ps
        prologue(p)
        p.dma(sb["idsb"][:], t["iden"])
        if "cmask" in t:
            for m in range(4):
                p.dma(sb["msb"][:, m * 512:(m + 1) * 512], t["cmask"][m])
        emit_fn(p, t)
        p.emit(block, sem)
    return nc


def io_specs(extra):
    base = [("xin", (NF, 128, T), F32, "ExternalInput"),
            ("xout", (NF, 128, T), F32, "ExternalOutput"),
            ("iden", (128, 128), F32, "ExternalInput"),
            ("xn", (NF, 128, T), F32, "Internal")]
    return base + extra


AI = "ExternalInput"
IN = "Internal"


def emit_blk0(p, t):
    ln_fm(p, t["xin"], t["xn"])
    trans_fm_tm(p, t["xn"], t["xn_tm"])
    mm_fm(p, t["xn"], 8, [
        dict(w=t["w1"], nK=8, O=4096, out=t["ffa"], ascale=t["s1"], func=AF.Silu)])
    mm_fm(p, t["xn"], 8, [
        dict(w=t["w2"], nK=8, O=4096, out=t["ffh"], ascale=t["s2"], resid=(t["ffa"], ALU.mult))])
    mm_fm(p, t["ffh"], 32, [
        dict(w=t["h1"], nK=32, O=4096, out=t["ffa"], ascale=t["sh1"], func=AF.Gelu)])
    mm_fm(p, t["ffa"], 32, [
        dict(w=t["h2"], nK=32, O=4096, out=t["ffh"], ascale=t["sh2"], func=AF.Gelu)])
    mm_fm(p, t["ffh"], 32, [
        dict(w=t["w3"], nK=32, O=1024, out=t["xtmp"], ascale=t["s3"], resid=(t["xin"], ALU.add))])
    mm_fm(p, t["xn"], 8, [
        dict(w=t["wq"], nK=8, O=1024, out=t["qd"], bias=t["bq"], resid=(t["xn"], ALU.add)),
        dict(w=t["wk0"], nK=8, O=1024, out=t["k0"], mode="tm", bias_row=t["bk0r"], resid=(t["xn_tm"], ALU.add)),
        dict(w=t["wk1"], nK=8, O=1024, out=t["k1"], mode="tm", bias_row=t["bk1r"], resid=(t["xn_tm"], ALU.add)),
        dict(w=t["wv0"], nK=8, O=1024, out=t["v0"], mode="tm", bias_row=t["bv0r"], resid=(t["xn_tm"], ALU.add)),
        dict(w=t["wv1"], nK=8, O=1024, out=t["v1"], mode="tm", bias_row=t["bv1r"], resid=(t["xn_tm"], ALU.add)),
    ])
    def silu05(p2, bufs, ft):
        p2.sop(lambda e: e.activation(bufs[0], bufs[0], AF.Silu))
        p2.vop(lambda e: e.tensor_scalar(bufs[0], bufs[0], 0.5, None, ALU.add))
    ew_fm(p, [t["qd"]], t["qd"], silu05)
    headnorm_fm(p, t["qd"], t["qh"])
    kst_tm(p, t["k0"], t["k0"])
    kst_tm(p, t["k1"], t["k1"])
    states_out(p, t["k0"], t["v0"], t["k1"], t["v1"], t["qh"], t["ap"])
    ln_fm(p, t["ap"], t["an"], gamma=t["lng"], beta=t["lnb"])
    mm_fm(p, t["an"], 8, [
        dict(w=t["wo"], nK=8, O=1024, out=t["xout"], bias=t["bo"], resid=(t["xtmp"], ALU.add))])


def emit_blk1(p, t):
    ln_fm(p, t["xin"], t["xn"])
    ln_fm(p, t["xn"], t["xnr"], rms=True, eps=1e-6)
    trans_fm_tm(p, t["xn"], t["xn_tm"])
    mm_fm(p, t["xnr"], 8, [
        dict(w=t["wq"], nK=8, O=1024, out=t["qd"], wscale=t["nw"]),
        dict(w=t["wk"], nK=8, O=1024, out=t["kd"], wscale=t["nw"]),
        dict(w=t["wv"], nK=8, O=1024, out=t["vtm"], mode="tm", wscale=t["nw"]),
        dict(w=t["wu"], nK=8, O=1024, out=t["ud"], koff=8),
    ], src2=t["xn"], nK2=8)
    def elup1(p2, bufs, ft):
        st2 = p2.sb["stage2"]
        p2.sop(lambda e: e.activation(st2[:, 0:2048], bufs[0], AF.Relu, scale=-1.0))
        p2.sop(lambda e: e.activation(st2[:, 0:2048], st2[:, 0:2048], AF.Exp, scale=-1.0))
        p2.sop(lambda e: e.activation(bufs[0], bufs[0], AF.Relu))
        p2.vop(lambda e: e.tensor_tensor(bufs[0], bufs[0], st2[:, 0:2048], ALU.add))
    ew_fm(p, [t["qd"]], t["qd"], elup1)
    ew_fm(p, [t["kd"]], t["kd"], elup1)
    sb = p.sb
    st = sb["stage"]
    p.dma(sb["small2"][:, 0:512], t["cind"])
    p.vop(lambda e: e.tensor_copy(sb["wsb"][:, 3584:4096], sb["small2"][:, 0:512]))
    for dt in range(8):
        for tt in range(NT):
            p.dma(st[:, 0:128], t["xn_tm"][tt][:, dt * 128:(dt + 1) * 128])
            p.vop(lambda e: e.tensor_copy(sb["actres"][:, 0:128], st[:, 0:128]))
            p.mm(lambda e, dt=dt, tt=tt: e.matmul(
                p.ps[5][:, 0:16], sb["actres"][:, 0:128],
                sb["wsb"][:, 3584 + tt * 16:3584 + (tt + 1) * 16],
                start=(tt == 0), stop=(tt == NT - 1)))
        p.vop(lambda e: e.tensor_copy(sb["osb"][:, 0:16], p.ps[5][:, 0:16]))
        p.vop(lambda e, dt=dt: e.tensor_copy(sb["growb"][0:64, 3968 + dt * 16:3968 + (dt + 1) * 16] if False else sb["wsb"][:, 3456 + dt * 16:3456 + (dt + 1) * 16], sb["osb"][:, 0:16]))
    p.dma(sb["u2b"][:], t["u2"])
    p.dma(sb["bselb"][32:48, :], t["bsel"])
    gmb = sb["small2"]
    for tt in range(NT):
        half = tt % 2
        n = tt // 2
        for k in range(8):
            p.dma(st[:, 0:128], t["ud"][k][:, tt * 128:(tt + 1) * 128])
            p.vop(lambda e, k=k: e.tensor_copy(sb["actres"][:, k * 128:(k + 1) * 128], st[:, 0:128]))
            p.dma(st[:, 0:256], t["xn"][k][:, n * 256:n * 256 + 256])
            p.vop(lambda e, k=k: e.tensor_copy(sb["actres"][:, 1024 + k * 256:1024 + (k + 1) * 256], st[:, 0:256]))
        def sc_grp(e):
            ins = None
            for k in range(8):
                ins = e.matmul(p.ps[0][:, 0:16], sb["actres"][:, k * 128:(k + 1) * 128],
                               sb["wsb"][:, 3456 + k * 16:3456 + (k + 1) * 16],
                               start=(k == 0), stop=(k == 7))
            return ins
        p.mm(sc_grp)
        def g_grp(e):
            ins = None
            for k in range(8):
                ins = e.matmul(p.ps[1][:, 0:256], sb["actres"][:, k * 128:(k + 1) * 128],
                               sb["actres"][:, 1024 + k * 256:1024 + (k + 1) * 256],
                               start=(k == 0), stop=(k == 7))
            return ins
        p.mm(g_grp)
        p.dma(gmb[:, 0:512], t["gm"])
        p.vop(lambda e, half=half: e.tensor_tensor(sb["stage2"][:, 0:256] if False else sb["osb"][:, 0:256], p.ps[1][:, 0:256], gmb[:, half * 256:half * 256 + 256], ALU.mult))
        p.vop(lambda e: e.tensor_reduce(sb["small"][:, 20:21], sb["osb"][:, 0:256], axis=AX.X, op=ALU.add))
        p.dma(sb["small"][:, 24:26], t["cntr"])
        p.vop(lambda e, half=half: e.tensor_scalar(sb["small"][:, 16:17], sb["small"][:, 20:21], sb["small"][:, 24 + half:25 + half], None, ALU.mult))
        p.sop(lambda e: e.activation(sb["small"][:, 0:16], p.ps[0][:, 0:16], AF.Copy, scale=1.0 / 32.0))
        c = tt // 2
        if c < 16:
            p.vop(lambda e, c=c: e.memset(sb["small"][:, c:16], -1e30))
        p.vop(lambda e: e.tensor_reduce(sb["small"][:, 17:18], sb["small"][:, 0:17], axis=AX.X, op=ALU.max, negate=True))
        p.sop(lambda e: e.activation(sb["small"][:, 0:17], sb["small"][:, 0:17], AF.Exp, bias=sb["small"][:, 17:18]))
        p.vop(lambda e: e.tensor_reduce(sb["small"][:, 18:19], sb["small"][:, 0:17], axis=AX.X, op=ALU.add))
        p.vop(lambda e: e.reciprocal(sb["small"][:, 19:20], sb["small"][:, 18:19]))
        p.vop(lambda e: e.tensor_scalar(sb["small"][:, 0:17], sb["small"][:, 0:17], sb["small"][:, 19:20], None, ALU.mult))
        p.mm(lambda e: e.transpose(p.ps[2][0:17, 0:128], sb["small"][:, 0:17], sb["idsb"][:]))
        p.vop(lambda e, tt=tt: e.tensor_copy(sb["growb"][0:17, tt * 128:(tt + 1) * 128], p.ps[2][0:17, 0:128]))
    for qc in range(8):
        p.mm(lambda e, qc=qc: e.matmul(p.ps[3][0:16, :], sb["u2b"][:],
                                       sb["growb"][0:17, qc * 512:(qc + 1) * 512],
                                       start=True, stop=True))
        p.vop(lambda e, qc=qc: e.tensor_copy(sb["growb"][32:48, qc * 512:(qc + 1) * 512],
                                             p.ps[3][0:16, :]))
    attn_qk(p, t["qd"], t["kd"], t["vtm"], t["ao"], 1.0, True, mode="linear",
            denom_eps=1e-6, sgfm=sb["growb"], bsel=sb["bselb"], hret=t["hr"])
    mm_fm(p, t["ao"], 8, [
        dict(w=t["wo"], nK=8, O=1024, out=t["xo1"], resid=(t["xn"], ALU.add))])
    gate_moe(p, t["xn"], t["gw"], 8, "top1w")
    for e_i in range(8):
        mm_fm(p, t["xn"], 8, [
            dict(w=t["w1e"][e_i], nK=8, O=4096, out=t["ffh"], bias=t["b1e"][e_i], func=AF.Relu)])
        mm_fm(p, t["ffh"], 32, [
            dict(w=t["w2e"][e_i], nK=32, O=1024, out=t["moed"], bias=t["b2e"][e_i], coef_e=e_i,
                 resid=(t["moed"], ALU.add) if e_i > 0 else None)])
    sb = p.sb
    def fin(p2, bufs, ft):
        b_xo1, b_hr, b_xin, b_moe = bufs
        p2.dma(sb["brow"][0:1, 0:512], sb["growb"][16:17, fin.c0:fin.c0 + 512], eng="sync")
        p2.mm(lambda e: e.matmul(p2.ps[4][:], sb["onesrow_b"][:], sb["brow"][0:1, 0:512],
                                 start=True, stop=True))
        p2.vop(lambda e: e.tensor_tensor(b_xo1, b_xo1, p2.ps[4][:], ALU.mult))
        p2.vop(lambda e: e.tensor_tensor(b_xo1, b_xo1, b_hr, ALU.add))
        p2.vop(lambda e: e.tensor_tensor(b_xo1, b_xo1, b_xin, ALU.add))
        p2.vop(lambda e: e.tensor_tensor(b_xo1, b_xo1, b_moe, ALU.add))
    for ft in range(NF):
        for c0 in range(0, T, 512):
            bufs = []
            for i, s in enumerate([t["xo1"], t["hr"], t["xin"], t["moed"]]):
                r = [sb["stage"], sb["stage2"], sb["small"], sb["small2"]][i][:, 0:512]
                p.dma(r, s[ft][:, c0:c0 + 512])
                bufs.append(r)
            fin.c0 = c0
            fin(p, bufs, ft)
            p.dma(t["xout"][ft][:, c0:c0 + 512], bufs[0])


def emit_blk2(p, t):
    ln_fm(p, t["xn_"] if False else t["xin"], t["xn"])
    mm_fm(p, t["xn"], 8, [
        dict(w=t["wq"], nK=8, O=1024, out=t["qd"]),
        dict(w=t["wkvd"], nK=8, O=256, out=t["lat"]),
        dict(w=t["baseW"], nK=8, O=1024, out=t["acc2"], bias=t["baseb"]),
        dict(w=t["shA"], nK=8, O=16, out=t["s1d"]),
    ])
    mm_fm(p, t["lat"], 2, [
        dict(w=t["wku"], nK=2, O=1024, out=t["kd"]),
        dict(w=t["wvu"], nK=2, O=1024, out=t["vtm"], mode="tm"),
    ])
    attn_qk(p, t["qd"], t["kd"], t["vtm"], t["ao"], 0.125, True, mode="softmax")
    mm_fm(p, t["ao"], 8, [
        dict(w=t["wout"], nK=8, O=1024, out=t["xa"], resid=(t["xin"], ALU.add))])
    gate_moe(p, t["xn"], t["rw"], 6, "top2renorm", bias_row=t["rb"])
    mm_fm(p, t["s1d"], 1, [
        dict(w=t["shB"], nK=1, kp=16, O=1024, out=t["acc2"], resid=(t["acc2"], ALU.add))])
    sb = p.sb
    for e_i in range(6):
        mm_fm(p, t["xn"], 8, [
            dict(w=t["rA"][e_i], nK=8, O=16, out=t["s1d"], coef_e=e_i)])
        mm_fm(p, t["s1d"], 1, [
            dict(w=t["rB"][e_i], nK=1, kp=16, O=1024, out=t["acc2"], resid=(t["acc2"], ALU.add))])
    def fin(p2, bufs, ft):
        p2.vop(lambda e: e.tensor_tensor(bufs[0], bufs[0], bufs[1], ALU.add))
    ew_fm(p, [t["xa"], t["acc2"]], t["xout"], fin)


def emit_blk3(p, t):
    sb = p.sb
    ln_fm(p, t["xin"], t["xn"])
    mm_fm(p, t["xn"], 8, [
        dict(w=t["wqa"], nK=8, O=1024, out=t["qd"]),
        dict(w=t["wka"], nK=8, O=1024, out=t["kd"]),
        dict(w=t["wva"], nK=8, O=1024, out=t["vtm"], mode="tm"),
    ])
    l2n_fm(p, t["qd"], t["qd"])
    l2n_fm(p, t["kd"], t["kd"])
    p.dma(sb["small2"][0:16, 0:1], t["temp"])
    p.vop(lambda e: e.reciprocal(sb["small2"][0:16, 1:2], sb["small2"][0:16, 0:1]))
    p.sop(lambda e: e.activation(sb["small2"][0:16, 1:2], sb["small2"][0:16, 1:2],
                                 AF.Copy, scale=1.0 / 64.0))
    p.dma(sb["small"][0:16, 0:1024], t["tsel"])
    p.vop(lambda e: e.tensor_copy(sb["growb"][0:16, 0:1024], sb["small"][0:16, 0:1024]))
    for ft in range(NF):
        p.vop(lambda e: e.tensor_copy(sb["osb"][0:16, 0:1], sb["small2"][0:16, 1:2]))
        p.mm(lambda e, ft=ft: e.matmul(p.ps[6][:, 0:1], sb["small"][0:16, ft * 128:(ft + 1) * 128],
                                       sb["osb"][0:16, 0:1], start=True, stop=True))
        p.vop(lambda e: e.tensor_copy(sb["small"][:, 1020:1021], p.ps[6][:, 0:1]))
        for cc in range(2):
            p.dma(sb["stage"][:, 0:2048], t["qd"][ft][:, cc * 2048:(cc + 1) * 2048])
            p.vop(lambda e: e.tensor_scalar(sb["stage"][:, 0:2048], sb["stage"][:, 0:2048],
                                            sb["small"][:, 1020:1021], None, ALU.mult))
            p.dma(t["qd"][ft][:, cc * 2048:(cc + 1) * 2048], sb["stage"][:, 0:2048])
    attn_qk(p, t["qd"], t["kd"], t["vtm"], t["ao"], 1.0, False, mode="softmax")
    mm_fm(p, t["ao"], 8, [
        dict(w=t["wproj"], nK=8, O=1024, out=t["xa"], bias=t["bproj"], resid=(t["xin"], ALU.add))])
    ln_fm(p, t["xn"], t["rn1"], rms=True, eps=1e-6, gamma=t["n1w"])
    mm_fm(p, t["rn1"], 8, [
        dict(w=t["wqm"], nK=8, O=1024, out=t["qm"]),
        dict(w=t["wdkv"], nK=8, O=512, out=t["latm"]),
    ])
    ln_fm(p, t["latm"], t["latm"], nF=4, eps=1e-5, gamma=t["lng3"], beta=t["lnb3"])
    mm_fm(p, t["latm"], 4, [
        dict(w=t["wuk"], nK=4, O=1024, out=t["km"]),
        dict(w=t["wuv"], nK=4, O=1024, out=t["vm"], mode="tm"),
    ])
    rope_fm(p, t["qm"], t["cs"])
    rope_fm(p, t["km"], t["cs"])
    attn_qk(p, t["qm"], t["km"], t["vm"], t["ao"], 0.125, True, mode="softmax")
    mm_fm(p, t["ao"], 8, [
        dict(w=t["wom"], nK=8, O=1024, out=t["x13"], resid=(t["xn"], ALU.add))])
    ln_fm(p, t["x13"], t["rn2"], rms=True, eps=1e-6, gamma=t["n2w"])
    gate_moe(p, t["rn2"], t["gw3"], 4, "top1", bias_row=t["gb3"])
    for e_i in range(4):
        mm_fm(p, t["rn2"], 8, [
            dict(w=t["w1m"][e_i], nK=8, O=4096, out=t["ffh"], bias=t["b1c"][e_i], func=AF.Relu)])
        mm_fm(p, t["ffh"], 32, [
            dict(w=t["w2m"][e_i], nK=32, O=1024, out=t["moed"], bias=t["b2c"][e_i], coef_e=e_i,
                 resid=(t["moed"], ALU.add) if e_i > 0 else None)])
    def fin(p2, bufs, ft):
        p2.vop(lambda e: e.tensor_tensor(bufs[0], bufs[0], bufs[1], ALU.add))
        p2.vop(lambda e: e.tensor_tensor(bufs[0], bufs[0], bufs[2], ALU.add))
    ew_fm(p, [t["xa"], t["x13"], t["moed"]], t["xout"], fin, ch=1024)



BF = ml_dtypes.bfloat16

def _causal_masks():
    m = np.zeros((4, 128, 512), dtype=np.float32)
    for mi in range(4):
        for i in range(128):
            m[mi, i, mi * 128 + i:] = 1.0
    return m.astype(BF)


def _io_w(W, nK):
    return np.ascontiguousarray(np.asarray(W, dtype=np.float32).reshape(nK, 128, -1))


def prep_blk0(I):
    f32 = np.float32
    h = dict(
        iden=np.eye(128, dtype=f32),
        wq=fm_w(I["L0_attn_Wq"]), bq=col(I["L0_attn_bq"]),
        wk0=fm_w(I["L0_attn_Wk"][0]), wk1=fm_w(I["L0_attn_Wk"][1]),
        wv0=fm_w(I["L0_attn_Wv"][0]), wv1=fm_w(I["L0_attn_Wv"][1]),
        bk0r=row(I["L0_attn_bk"][0]), bk1r=row(I["L0_attn_bk"][1]),
        bv0r=row(I["L0_attn_bv"][0]), bv1r=row(I["L0_attn_bv"][1]),
        wo=fm_w(I["L0_attn_Wo"]), bo=col(I["L0_attn_bo"]),
        lng=col(I["L0_attn_ln_g"]), lnb=col(I["L0_attn_ln_b"]),
        w1=fm_w(I["L0_ffn_w1_w"]), s1=col(I["L0_ffn_w1_s"]),
        w2=fm_w(I["L0_ffn_w2_w"]), s2=col(I["L0_ffn_w2_s"]),
        h1=fm_w(I["L0_ffn_h1_w"]), sh1=col(I["L0_ffn_h1_s"]),
        h2=fm_w(I["L0_ffn_h2_w"]), sh2=col(I["L0_ffn_h2_s"]),
        w3=fm_w(I["L0_ffn_w3_w"]), s3=col(I["L0_ffn_w3_s"]),
    )
    specs = [("xin", (NF, 128, T), F32, AI), ("xout", (NF, 128, T), F32, "ExternalOutput"),
             ("iden", (128, 128), F32, AI)]
    for n in ("xn", "xtmp", "qd", "qh", "ap", "an"):
        specs.append((n, (NF, 128, T), F32, IN))
    for n in ("xn_tm", "k0", "k1", "v0", "v1"):
        specs.append((n, (NT, 128, 1024), F32, IN))
    for n in ("ffa", "ffh"):
        specs.append((n, (32, 128, T), F32, IN))
    for n, a in h.items():
        if n != "iden":
            specs.append((n, a.shape, F32, AI))
    return h, specs


def prep_blk1(I):
    f32 = np.float32
    cind = np.zeros((32, 128, 16), dtype=f32)
    for tt in range(32):
        cind[tt, :, tt // 2] = 1.0 / 256.0
    gm = np.zeros((2, 128, 256), dtype=f32)
    cntr = np.zeros((2, 128, 1), dtype=f32)
    for half in range(2):
        for r in range(128):
            gm[half, r, :half * 128 + r + 1] = 1.0
            cntr[half, r, 0] = 1.0 / (32.0 * (half * 128 + r + 1))
    u2 = np.zeros((17, 16), dtype=f32)
    for m in range(16):
        u2[m, :m + 1] = 1.0
    bsel = np.zeros((16, 2048), dtype=f32)
    for n in range(16):
        bsel[n, n * 128:(n + 1) * 128] = 1.0
    h = dict(
        iden=np.eye(128, dtype=f32), cmask=_causal_masks(),
        cind=np.ascontiguousarray(cind.transpose(1, 0, 2).reshape(128, 512)),
        gm=np.ascontiguousarray(gm.transpose(1, 0, 2).reshape(128, 512)),
        cntr=np.ascontiguousarray(cntr.transpose(1, 0, 2).reshape(128, 2)),
        u2=u2.astype(BF), bsel=bsel.astype(BF),
        nw=col(I["L1_attn_norm_w"]),
        wq=fm_w(I["L1_attn_Wq"]), wk=fm_w(I["L1_attn_Wk"]),
        wv=fm_w(I["L1_attn_Wv"]), wo=fm_w(I["L1_attn_Wo"]),
        wu=fm_w(I["L1_attn_Wu"]),
        gw=fm_w(I["L1_ffn_gate_w"]),
    )
    for e in range(8):
        h["w1e%d" % e] = _io_w(I["L1_ffn_w1"][e], 8)
        h["b1e%d" % e] = col(I["L1_ffn_b1"][e])
        h["w2e%d" % e] = _io_w(I["L1_ffn_w2"][e], 32)
        h["b2e%d" % e] = col(I["L1_ffn_b2"][e])
    specs = [("xin", (NF, 128, T), F32, AI), ("xout", (NF, 128, T), F32, "ExternalOutput"),
             ("iden", (128, 128), F32, AI),
             ("cmask", (4, 128, 512), BF16, AI), ("cind", (128, 512), F32, AI),
             ("gm", (128, 512), F32, AI), ("cntr", (128, 2), F32, AI),
             ("u2", (17, 16), BF16, AI), ("bsel", (16, 2048), BF16, AI)]
    for n in ("xn", "xnr", "qd", "kd", "ud", "ao", "hr", "xo1", "moed"):
        specs.append((n, (NF, 128, T), F32, IN))
    for n in ("xn_tm", "vtm"):
        specs.append((n, (NT, 128, 1024), F32, IN))
    specs.append(("ffh", (32, 128, T), F32, IN))
    for n, a in h.items():
        if n not in ("iden", "cmask", "cind", "gm", "cntr", "u2", "bsel"):
            specs.append((n, a.shape, F32, AI))
    return h, specs


def prep_blk2(I):
    f32 = np.float32
    h = dict(
        iden=np.eye(128, dtype=f32), cmask=_causal_masks(),
        wq=_io_w(I["L2_attn_Wq"].reshape(1024, 1024), 8),
        wkvd=_io_w(I["L2_attn_Wkv_down"], 8),
        wku=_io_w(np.asarray(I["L2_attn_Wk_up"]).reshape(256, 1024), 2),
        wvu=_io_w(np.asarray(I["L2_attn_Wv_up"]).reshape(256, 1024), 2),
        wout=_io_w(np.asarray(I["L2_attn_Wout"]).reshape(1024, 1024), 8),
        baseW=fm_w(I["L2_ffn_base_W"]), baseb=col(I["L2_ffn_base_b"]),
        shA=fm_w(I["L2_ffn_shared_A"]),
        shB=fm_w(I["L2_ffn_shared_B"]),
        rw=fm_w(I["L2_ffn_router_W"]), rb=row(I["L2_ffn_router_b"]),
    )
    for e in range(6):
        h["rA%d" % e] = fm_w(I["L2_ffn_routed_A"][e])
        h["rB%d" % e] = fm_w(I["L2_ffn_routed_B"][e])
    specs = [("xin", (NF, 128, T), F32, AI), ("xout", (NF, 128, T), F32, "ExternalOutput"),
             ("iden", (128, 128), F32, AI), ("cmask", (4, 128, 512), BF16, AI)]
    for n in ("xn", "qd", "kd", "ao", "xa", "acc2"):
        specs.append((n, (NF, 128, T), F32, IN))
    specs.append(("lat", (2, 128, T), F32, IN))
    specs.append(("vtm", (NT, 128, 1024), F32, IN))
    specs.append(("s1d", (1, 16, T), F32, IN))
    for n, a in h.items():
        if n not in ("iden", "cmask"):
            specs.append((n, a.shape, F32, AI))
    return h, specs


def prep_blk3(I):
    f32 = np.float32
    qkv = np.asarray(I["L3_attn_qkv_w"], dtype=f32)
    tsel = np.zeros((16, 1024), dtype=f32)
    for hh in range(16):
        ft, hf = hh // 2, hh % 2
        tsel[hh, ft * 128 + hf * 64:ft * 128 + hf * 64 + 64] = 1.0
    pos = np.arange(T, dtype=np.float64)[None, :]
    invf = (1.0 / 10000.0 ** (np.arange(32, dtype=np.float64) / 32.0))[:, None]
    ang = (invf * pos)
    cs = np.stack([np.cos(ang), np.sin(ang)]).astype(f32)
    h = dict(
        iden=np.eye(128, dtype=f32), cmask=_causal_masks(),
        wqa=fm_w(qkv[0:1024]), wka=fm_w(qkv[1024:2048]), wva=fm_w(qkv[2048:3072]),
        temp=np.asarray(I["L3_attn_temperature"], dtype=f32).reshape(16, 1),
        tsel=tsel, cs=cs,
        wproj=fm_w(I["L3_attn_proj_w"]), bproj=col(I["L3_attn_proj_b"]),
        n1w=col(I["L3_ffn_norm1_w"]), n2w=col(I["L3_ffn_norm2_w"]),
        wqm=fm_w(I["L3_ffn_Wq"]), wdkv=fm_w(I["L3_ffn_Wdkv"]),
        lng3=col(I["L3_ffn_ln_g"]), lnb3=col(I["L3_ffn_ln_b"]),
        wuk=fm_w(I["L3_ffn_Wuk"]), wuv=fm_w(I["L3_ffn_Wuv"]),
        wom=fm_w(I["L3_ffn_Wo"]),
        gw3=fm_w(I["L3_ffn_gate_w"]), gb3=row(I["L3_ffn_gate_b"]),
    )
    for e in range(4):
        h["w1m%d" % e] = fm_w(I["L3_ffn_W1"][e])
        h["b1c%d" % e] = col(I["L3_ffn_b1"][e])
        h["w2m%d" % e] = fm_w(I["L3_ffn_W2"][e])
        h["b2c%d" % e] = col(I["L3_ffn_b2"][e])
    specs = [("xin", (NF, 128, T), F32, AI), ("xout", (NF, 128, T), F32, "ExternalOutput"),
             ("iden", (128, 128), F32, AI), ("cmask", (4, 128, 512), BF16, AI),
             ("temp", (16, 1), F32, AI), ("tsel", (16, 1024), F32, AI),
             ("cs", (2, 32, T), F32, AI)]
    for n in ("xn", "qd", "kd", "ao", "xa", "rn1", "qm", "km", "x13", "rn2", "moed"):
        specs.append((n, (NF, 128, T), F32, IN))
    specs.append(("latm", (4, 128, T), F32, IN))
    for n in ("vtm", "vm"):
        specs.append((n, (NT, 128, 1024), F32, IN))
    specs.append(("ffh", (32, 128, T), F32, IN))
    for n, a in h.items():
        if n not in ("iden", "cmask", "temp", "tsel", "cs"):
            specs.append((n, a.shape, F32, AI))
    return h, specs


def _listify(t, base, n):
    return [t["%s%d" % (base, e)] for e in range(n)]


def _emit1(p, t):
    t["w1e"] = _listify(t, "w1e", 8)
    t["b1e"] = _listify(t, "b1e", 8)
    t["w2e"] = _listify(t, "w2e", 8)
    t["b2e"] = _listify(t, "b2e", 8)
    emit_blk1(p, t)


def _emit2(p, t):
    t["rA"] = _listify(t, "rA", 6)
    t["rB"] = _listify(t, "rB", 6)
    emit_blk2(p, t)


def _emit3(p, t):
    t["w1m"] = _listify(t, "w1m", 4)
    t["b1c"] = _listify(t, "b1c", 4)
    t["w2m"] = _listify(t, "w2m", 4)
    t["b2c"] = _listify(t, "b2c", 4)
    emit_blk3(p, t)


_PROGS = {}
_PREPS = [(prep_blk0, emit_blk0), (prep_blk1, _emit1), (prep_blk2, _emit2), (prep_blk3, _emit3)]


def kernel(**inputs):
    xs = np.asarray(inputs["L0_attn_x"], dtype=np.float32)
    cores = [fmx(xs[b]) for b in range(8)]
    for bi, (prep, emit) in enumerate(_PREPS):
        h, specs = prep(inputs)
        if bi not in _PROGS:
            _PROGS[bi] = make_prog(specs, emit)
        nc = _PROGS[bi]
        in_maps = [dict(h, xin=cores[b]) for b in range(8)]
        res = run_bass_kernel_spmd(nc, in_maps, core_ids=list(range(8)))
        cores = [np.asarray(r["xout"], dtype=np.float32) for r in res.results]
    return np.stack([c.reshape(1024, T).T for c in cores]).astype(np.float32)
